# version 06 (final)
# baseline (speedup 1.0000x reference)
import contextlib
import numpy as np
import ml_dtypes
import concourse.bass as bass
import concourse.mybir as mybir
from concourse.bass_utils import run_bass_kernel_spmd

F32 = mybir.dt.float32
I32 = mybir.dt.int32
BF16 = mybir.dt.bfloat16
AF = mybir.ActivationFunctionType
OP = mybir.AluOpType
AX = mybir.AxisListType.X
bf16 = ml_dtypes.bfloat16

S = 2048
DM = 4096
DH = 128
NHH = 16
CHH = 2048
NST = 16
NSTH = 8
PHASES = (1, 2, 5, 6, 7)
P7_QLOOP = True
P7_NQ = 4


def build_nc(part):
    nc = bass.Bass("TRN2", target_bir_lowering=False)

    x_in = nc.dram_tensor("x", [S, DM], F32, kind="ExternalInput").ap()
    xh_in = nc.dram_tensor("x_half", [NSTH * 128, DM], F32, kind="ExternalInput").ap()
    w_n = {k: nc.dram_tensor(f"w{k}_n", [CHH, DM], F32, kind="ExternalInput").ap()
           for k in ("q", "k", "v")}
    w_t = {k: nc.dram_tensor(f"w{k}_t", [DM, CHH], F32, kind="ExternalInput").ap()
           for k in ("q", "k", "v")}
    wo_n = nc.dram_tensor("wo_n", [DM, DM], F32, kind="ExternalInput").ap()
    wo_t = nc.dram_tensor("wo_t", [DM, DM], F32, kind="ExternalInput").ap()
    id_t = nc.dram_tensor("id_t", [DM, DM], F32, kind="ExternalInput").ap()
    cos_in = nc.dram_tensor("cos_t", [S, DH], F32, kind="ExternalInput").ap()
    ssin_in = nc.dram_tensor("ssin_t", [S, DH], F32, kind="ExternalInput").ap()
    dmask_in = nc.dram_tensor("dmask", [128, 128], F32, kind="ExternalInput").ap()
    eyeb_in = nc.dram_tensor("eye_b", [128, 128], BF16, kind="ExternalInput").ap()
    eyef_in = nc.dram_tensor("eye_f", [128, 128], F32, kind="ExternalInput").ap()
    ones_in = nc.dram_tensor("ones_f", [1, 128], F32, kind="ExternalInput").ap()
    y_out = nc.dram_tensor("y", [NSTH * 128, DM], F32, kind="ExternalOutput").ap()

    xint_h = nc.dram_tensor("xint_h", [S, DM], BF16, kind="Internal").ap()
    q_h = nc.dram_tensor("q_h", [S, NHH, DH], BF16, kind="Internal").ap()
    k_h = nc.dram_tensor("k_h", [S, NHH, DH], BF16, kind="Internal").ap()
    v_h = nc.dram_tensor("v_h", [S, NHH, DH], BF16, kind="Internal").ap()
    sc_h = nc.dram_tensor("sc_h", [2 * DM], F32, kind="Internal").ap()
    g_hbm = nc.dram_tensor("g_h", [NSTH * 128, DM], BF16, kind="Internal").ap()
    if part == 1:
        cc_i = nc.dram_tensor("attn", [S, CHH], F32, kind="ExternalOutput").ap()
        attn_in = None
    else:
        cc_i = None
        attn_in = nc.dram_tensor("attn_full", [NSTH * 128, DM], F32,
                                 kind="ExternalInput").ap()

    stack = contextlib.ExitStack()
    _alloc_cache = {}
    sems = {}
    for name in ("din", "dst", "dsc", "vv", "ss", "tt", "bar", "cc"):
        sems[name] = stack.enter_context(nc.semaphore(name))

    sb = {}
    def res_sb(name, shape, dt):
        sb[name] = stack.enter_context(nc.sbuf_tensor("sb_" + name, shape, dt))
    res_sb("eye_b", [128, 128], BF16)
    res_sb("eye_f", [128, 128], F32)
    res_sb("ones", [1, 128], F32)
    res_sb("dmask", [128, 128], F32)
    res_sb("scx", [128, NST], F32)
    res_sb("cq", [128, NST * NHH], F32)
    res_sb("Lc", [128, 4], F32)
    res_sb("invz", [128, NST], F32)
    res_sb("sca", [128, NSTH], F32)
    res_sb("tiny", [128, 32], F32)
    res_sb("wsc", [128, 64], F32)
    res_sb("srow", [1, DM], F32)

    _DRAIN_OPS = {
        "tensor_copy", "tensor_tensor", "tensor_scalar", "tensor_single_scalar",
        "scalar_tensor_tensor", "tensor_reduce", "reduce_sum", "reduce_max",
        "reciprocal", "memset", "activation", "copy", "mul", "add", "sqrt",
        "square", "sign", "tensor_add", "tensor_sub", "tensor_mul", "tensor_max",
        "select", "tensor_scalar_mul", "tensor_scalar_add",
    }
    _TRACK_OPS = _DRAIN_OPS | {"matmul", "transpose", "ldweights"}

    class _DrainProxy:
        def __init__(self, real, do_drain):
            self._real = real
            self._drain = do_drain
            self.last = None

        def __getattr__(self, n):
            a = getattr(self._real, n)
            if n in _TRACK_OPS and callable(a):
                px = self

                def wrapped(*args, **kw):
                    r = a(*args, **kw)
                    if px._drain and n != "matmul":
                        r = px._real.drain()
                    px.last = r
                    return r
                return wrapped
            return a

    def emit_all(eng, e, part=part):
        e = _DrainProxy(e, eng in ("vector", "scalar"))
        ctr = dict(din=0, dst=0, dsc=0, vv=0, ss=0, tt=0, bar=0, cc=0)
        nbar = [0]
        uid = [0]

        class _Reuse:
            def __init__(self, key):
                self.key = key

            def __enter__(self):
                return _alloc_cache[self.key]

            def __exit__(self, *a):
                return False

        def sbt(shape, dt):
            uid[0] += 1
            key = ("s", uid[0])
            if eng == "sync":
                cm = nc.sbuf_tensor(f"t{uid[0]}", shape, dt)

                class _CM:
                    def __enter__(self_):
                        h = cm.__enter__()
                        _alloc_cache[key] = h
                        return h

                    def __exit__(self_, *a):
                        return cm.__exit__(*a)
                return _CM()
            return _Reuse(key)

        def pst(shape, dt=F32):
            uid[0] += 1
            key = ("p", uid[0])
            if eng == "sync":
                cm = nc.psum_tensor(f"p{uid[0]}", shape, dt)

                class _CM:
                    def __enter__(self_):
                        h = cm.__enter__()
                        _alloc_cache[key] = h
                        return h

                    def __exit__(self_, *a):
                        return cm.__exit__(*a)
                return _CM()
            return _Reuse(key)

        def inc(sn, n):
            ctr[sn] += n
            return ctr[sn]

        def ld(dst_ap, src_ap):
            v = inc("din", 16)
            if eng == "sync":
                e.dma_start(dst_ap, src_ap).then_inc(sems["din"], 16)
                e.wait_ge(sems["din"], v)
            return v

        def st(dst_ap, src_ap, q="gpsimd"):
            key = "dst" if q == "gpsimd" else "dsc"
            v = inc(key, 16)
            if eng == q:
                e.dma_start(dst_ap, src_ap).then_inc(sems[key], 16)
                e.wait_ge(sems[key], v)
            return v

        def w(sn, val, which):
            if eng == which:
                e.wait_ge(sems[sn], val)

        def bump(sn, which):
            v = inc(sn, 1)
            if eng == which:
                li = e.last
                e.last = None
                if li is not None:
                    li.then_inc(sems[sn], 1)
                else:
                    e.sem_inc(sems[sn], 1)
            return v

        def barrier():
            nbar[0] += 1
            e.drain()
            e.sem_inc(sems["bar"], 1)
            e.wait_ge(sems["bar"], 5 * nbar[0])
            e.last = None

        V = eng == "vector"
        SC = eng == "scalar"
        T = eng == "tensor"
        tny = sb["tiny"]

        with (
            pst([128, 2048]) as ps_a,
            pst([128, 1024]) as ps_b,
            pst([128, 512], BF16) as ps_t1,
            pst([128, 512], BF16) as ps_t2,
        ):
            ps_t = [ps_t1, ps_t2]

            ld(sb["eye_b"][:, :], eyeb_in)
            ld(sb["eye_f"][:, :], eyef_in)
            ld(sb["ones"][:, :], ones_in)
            ld(sb["dmask"][:, :], dmask_in)
            w("din", ctr["din"], "vector")
            if V:
                e.memset(tny[:, 0:1], 127.0)
                e.reciprocal(sb["Lc"][:, 1:2], tny[:, 0:1])
                e.memset(tny[:, 1:2], float(np.float32(np.sqrt(np.float32(128.0)))))
                e.reciprocal(sb["Lc"][:, 2:3], tny[:, 1:2])
                e.memset(tny[:, 2:3], 7.0)
                e.reciprocal(sb["Lc"][:, 3:4], tny[:, 2:3])
            bump("vv", "vector")
            w("vv", ctr["vv"], "scalar")
            if SC:
                e.activation(sb["Lc"][:, 0:1], tny[:, 0:1], AF.Ln)
            bump("ss", "scalar")

            def _p1():
                with (
                    sbt([128, DM], F32) as xt0,
                    sbt([128, DM], F32) as xt1,
                    sbt([128, DM], F32) as cen,
                    sbt([128, DM], F32) as wa,
                    sbt([128, DM], F32) as wb,
                    sbt([128, DM], BF16) as ib0,
                    sbt([128, DM], BF16) as ib1,
                ):
                    free_at = {}
                    for i in range(NST):
                        xt = xt0 if i % 2 == 0 else xt1
                        ib = ib0 if i % 2 == 0 else ib1
                        if i >= 2:
                            w("vv", free_at[i - 2], "sync")
                        vl = ld(xt[:, :], x_in[i * 128:(i + 1) * 128, :])
                        w("din", vl, "vector")
                        if V:
                            e.reduce_sum(tny[:, 1:2], xt[:, :], AX)
                            e.tensor_single_scalar(tny[:, 2:3], tny[:, 1:2],
                                                   -1.0 / float(DM), OP.mult)
                        bump("vv", "vector")
                        w("vv", ctr["vv"], "scalar")
                        if SC:
                            e.activation(cen[:, :], xt[:, :], AF.Identity,
                                         bias=tny[:, 2:3])
                            e.activation(wa[:, :], cen[:, :], AF.Square,
                                         accum_out=tny[:, 3:4])
                        bump("ss", "scalar")
                        w("ss", ctr["ss"], "vector")
                        if V:
                            e.tensor_scalar(tny[:, 4:5], tny[:, 3:4], 1.0 / float(DM),
                                            1e-5, OP.mult, OP.add)
                        bump("vv", "vector")
                        w("vv", ctr["vv"], "scalar")
                        if SC:
                            e.activation(tny[:, 5:6], tny[:, 4:5], AF.Sqrt)
                        bump("ss", "scalar")
                        w("ss", ctr["ss"], "vector")
                        if V:
                            e.reciprocal(tny[:, 6:7], tny[:, 5:6])
                            e.reduce_max(tny[:, 7:8], cen[:, :], AX,
                                         apply_absolute_value=True)
                            e.tensor_scalar(tny[:, 8:9], tny[:, 7:8], tny[:, 6:7],
                                            sb["Lc"][:, 1:2], OP.mult, OP.mult)
                            e.tensor_scalar(sb["scx"][:, i:i + 1], tny[:, 8:9], 1e-8,
                                            None, OP.max)
                            e.reciprocal(tny[:, 9:10], tny[:, 7:8])
                            e.tensor_scalar(wa[:, :], cen[:, :], 127.0, tny[:, 9:10],
                                            OP.mult, OP.mult)
                            e.tensor_copy(wb[:, :].bitcast(I32), wa[:, :])
                        if i >= 2:
                            w("dst", 16 * (i - 1), "vector")
                        if V:
                            e.tensor_copy(ib[:, :], wb[:, :].bitcast(I32))
                        fv = bump("vv", "vector")
                        free_at[i] = fv
                        w("vv", fv, "gpsimd")
                        st(xint_h[i * 128:(i + 1) * 128, :], ib[:, :])
                w("dst", ctr["dst"], "gpsimd")
                barrier()
            if part == 1 and 1 in PHASES:
                _p1()
            def _p2():
                with (
                    sbt([128, 32 * CHH], BF16) as wint,
                    sbt([128, CHH], F32) as wstr0,
                    sbt([128, CHH], F32) as wstr1,
                    sbt([128, CHH], F32) as scB,
                    sbt([128, CHH], F32) as invB,
                    sbt([128, DM], BF16) as xn_sb,
                    sbt([128, 128], BF16) as xTd0,
                    sbt([128, 128], BF16) as xTd1,
                    sbt([128, CHH], F32) as out_sb,
                    sbt([128, CHH], BF16) as int_sb,
                    sbt([128, CHH], BF16) as dq_sb,
                    sbt([128, 32], F32) as smax,
                    sbt([128, 128], F32) as cos_sb,
                    sbt([128, 128], F32) as ssin_sb,
                ):
                    for wi, kind in enumerate(("q", "k", "v")):
                        wn_ap, wt_ap = w_n[kind], w_t[kind]
                        spill = {"q": q_h, "k": k_h, "v": v_h}[kind]
                        if wi > 0:
                            w("dst", ctr["dst"], "vector")
                        for ct in range(16):
                            if ct >= 1:
                                w("vv", ctr["vv"], "sync")
                            vl = ld(wstr0[:, :], wn_ap[ct * 128:(ct + 1) * 128, 0:2048])
                            vl2 = ld(wstr1[:, :], wn_ap[ct * 128:(ct + 1) * 128, 2048:4096])
                            w("din", vl, "vector")
                            if V:
                                e.reduce_max(tny[:, 9:10], wstr0[:, :], AX,
                                             apply_absolute_value=True)
                            bump("vv", "vector")
                            w("din", vl2, "vector")
                            if V:
                                e.reduce_max(tny[:, 10:11], wstr1[:, :], AX,
                                             apply_absolute_value=True)
                                e.tensor_tensor(tny[:, 11:12], tny[:, 9:10],
                                                tny[:, 10:11], OP.max)
                                e.tensor_scalar(sb["wsc"][:, ct:ct + 1], tny[:, 11:12],
                                                sb["Lc"][:, 3:4], 1e-8, OP.mult, OP.max)
                                e.reciprocal(sb["wsc"][:, 32 + ct:32 + ct + 1],
                                             sb["wsc"][:, ct:ct + 1])
                            fv = bump("vv", "vector")
                            w("vv", fv, "gpsimd")
                            st(sc_h[ct * 128:(ct + 1) * 128], sb["wsc"][:, ct:ct + 1])
                            st(sc_h[DM + ct * 128:DM + (ct + 1) * 128],
                               sb["wsc"][:, 32 + ct:32 + ct + 1])
                        w("dst", ctr["dst"], "sync")
                        vl = ld(sb["srow"][0:1, 0:2048], sc_h[0:2048])
                        vl = ld(sb["srow"][0:1, 2048:4096], sc_h[DM:DM + 2048])
                        w("din", vl, "tensor")
                        for cblk in range(4):
                            w("ss", ctr["ss"], "tensor")
                            if T:
                                e.matmul(ps_b[:, 0:512], sb["ones"][0:1, :],
                                         sb["srow"][0:1, cblk * 512:(cblk + 1) * 512],
                                         start=True, stop=True)
                            bump("tt", "tensor")
                            w("tt", ctr["tt"], "scalar")
                            if SC:
                                e.copy(scB[:, cblk * 512:(cblk + 1) * 512], ps_b[:, 0:512])
                            bump("ss", "scalar")
                            w("ss", ctr["ss"], "tensor")
                            if T:
                                e.matmul(ps_b[:, 0:512], sb["ones"][0:1, :],
                                         sb["srow"][0:1, 2048 + cblk * 512:2048 + (cblk + 1) * 512],
                                         start=True, stop=True)
                            bump("tt", "tensor")
                            w("tt", ctr["tt"], "scalar")
                            if SC:
                                e.copy(invB[:, cblk * 512:(cblk + 1) * 512], ps_b[:, 0:512])
                            bump("ss", "scalar")
                        evB = bump("ss", "scalar")
                        for rt in range(32):
                            ws = wstr0 if rt % 2 == 0 else wstr1
                            if rt >= 2:
                                w("vv", ctr["vv"], "sync")
                            vl = ld(ws[:, :], wt_ap[rt * 128:(rt + 1) * 128, :])
                            w("din", vl, "vector")
                            w("ss", evB, "vector")
                            if V:
                                e.tensor_tensor(out_sb[:, :], ws[:, :], invB[:, :], OP.mult)
                                e.tensor_copy(xn_sb[:, :].bitcast(I32), out_sb[:, :])
                                e.tensor_copy(wint[:, rt * CHH:(rt + 1) * CHH],
                                              xn_sb[:, :].bitcast(I32))
                            bump("vv", "vector")
                        wq_done = ctr["vv"]
                        for stt_i in range(NST):
                            w("vv", ctr["vv"], "sync")
                            w("ss", ctr["ss"], "tensor")
                            vl = ld(xn_sb[:, :], xint_h[stt_i * 128:(stt_i + 1) * 128, :])
                            if kind != "v":
                                vl2 = ld(cos_sb[:, :], cos_in[stt_i * 128:(stt_i + 1) * 128, :])
                                vl2 = ld(ssin_sb[:, :], ssin_in[stt_i * 128:(stt_i + 1) * 128, :])
                            w("din", ctr["din"], "tensor")
                            w("vv", wq_done, "tensor")
                            w("vv", ctr["vv"], "tensor")
                            evac = {}
                            def do_T(d):
                                if d >= 2:
                                    w("ss", evac[d - 2], "tensor")
                                if T:
                                    e.transpose(ps_t[d % 2][:, 0:128],
                                                xn_sb[:, d * 128:(d + 1) * 128],
                                                sb["eye_b"][:, :])
                                tv = bump("tt", "tensor")
                                w("tt", tv, "scalar")
                                if SC:
                                    e.copy((xTd0 if d % 2 == 0 else xTd1)[:, :],
                                           ps_t[d % 2][:, 0:128])
                                evac[d] = bump("ss", "scalar")
                            do_T(0)
                            for d in range(32):
                                if d + 1 < 32:
                                    do_T(d + 1)
                                w("ss", evac[d], "tensor")
                                for blk in range(4):
                                    if T:
                                        e.matmul(ps_a[:, blk * 512:(blk + 1) * 512],
                                                 (xTd0 if d % 2 == 0 else xTd1)[:, :],
                                                 wint[:, d * CHH + blk * 512:d * CHH + (blk + 1) * 512],
                                                 start=(d == 0), stop=(d == 31))
                                bump("tt", "tensor")
                            mm_done = ctr["tt"]
                            w("tt", mm_done, "vector")
                            if V:
                                for blk in range(4):
                                    e.scalar_tensor_tensor(
                                        out_sb[:, blk * 512:(blk + 1) * 512],
                                        ps_a[:, blk * 512:(blk + 1) * 512],
                                        sb["scx"][:, stt_i:stt_i + 1],
                                        scB[:, blk * 512:(blk + 1) * 512],
                                        OP.mult, OP.mult)
                            ep_done = bump("vv", "vector")
                            if kind != "v":
                                w("vv", ep_done, "scalar")
                                if SC:
                                    e.copy(wstr1[:, :].rearrange("p (h x) -> p h x", h=NHH)[:, :, 0:64],
                                           out_sb[:, :].rearrange("p (h x) -> p h x", h=NHH)[:, :, 64:128])
                                    e.copy(wstr1[:, :].rearrange("p (h x) -> p h x", h=NHH)[:, :, 64:128],
                                           out_sb[:, :].rearrange("p (h x) -> p h x", h=NHH)[:, :, 0:64])
                                sh_done = bump("ss", "scalar")
                                w("ss", sh_done, "vector")
                                cosx = cos_sb[:, :].unsqueeze(1).broadcast_to([128, NHH, 128])
                                ssx = ssin_sb[:, :].unsqueeze(1).broadcast_to([128, NHH, 128])
                                o3 = out_sb[:, :].rearrange("p (h x) -> p h x", h=NHH)
                                r3 = wstr1[:, :].rearrange("p (h x) -> p h x", h=NHH)
                                a3 = wstr0[:, :].rearrange("p (h x) -> p h x", h=NHH)
                                if V:
                                    e.tensor_tensor(a3, o3, cosx, OP.mult)
                                    e.tensor_tensor(r3, r3, ssx, OP.mult)
                                    e.tensor_tensor(o3, a3, r3, OP.add)
                                bump("vv", "vector")
                            if V:
                                for h in range(NHH):
                                    e.reduce_max(smax[:, h:h + 1],
                                                 out_sb[:, h * 128:(h + 1) * 128], AX,
                                                 apply_absolute_value=True)
                            bump("vv", "vector")
                            if V:
                                if kind == "q":
                                    e.tensor_scalar(sb["cq"][:, stt_i * NHH:(stt_i + 1) * NHH],
                                                    smax[:, 0:16], sb["Lc"][:, 1:2],
                                                    sb["Lc"][:, 2:3], OP.mult, OP.mult)
                                e.reciprocal(smax[:, 16:32], smax[:, 0:16])
                                smi = smax[:, 16:32].unsqueeze(2).broadcast_to([128, NHH, 128])
                                o3 = out_sb[:, :].rearrange("p (h x) -> p h x", h=NHH)
                                a3 = wstr0[:, :].rearrange("p (h x) -> p h x", h=NHH)
                                e.scalar_tensor_tensor(a3, o3, 127.0, smi, OP.mult, OP.mult)
                                e.tensor_copy(wstr1[:, :].bitcast(I32), wstr0[:, :])
                                if kind == "q":
                                    e.tensor_copy(int_sb[:, :], wstr1[:, :].bitcast(I32))
                                if kind != "q":
                                    d3 = dq_sb[:, :].rearrange("p (h x) -> p h x", h=NHH)
                                    z3 = wstr1[:, :].bitcast(I32).rearrange("p (h x) -> p h x", h=NHH)
                                    sm7 = smax[:, 0:16].unsqueeze(2).broadcast_to([128, NHH, 128])
                                    e.scalar_tensor_tensor(d3, z3, sb["Lc"][:, 1:2], sm7,
                                                           OP.mult, OP.mult)
                            qv = bump("vv", "vector")
                            w("vv", qv, "gpsimd")
                            src = int_sb if kind == "q" else dq_sb
                            if stt_i + 1 < NST:
                                w("dst", ctr["dst"] + 16, "vector")
                            st(spill[stt_i * 128:(stt_i + 1) * 128, :, :],
                               src[:, :].rearrange("p (h d) -> p h d", h=NHH))
                        w("dst", ctr["dst"], "gpsimd")
                        barrier()
            if part == 1 and 2 in PHASES:
                _p2()
            def _p5():
                with (
                    sbt([128, CHH], BF16) as q0, sbt([128, CHH], BF16) as q1,
                    sbt([128, CHH], BF16) as k0, sbt([128, CHH], BF16) as k1,
                    sbt([128, CHH], BF16) as v0, sbt([128, CHH], BF16) as v1,
                    sbt([128, CHH], BF16) as qT, sbt([128, CHH], BF16) as kT,
                    sbt([128, S], F32) as e_sb,
                    sbt([128, S], F32) as md_sb,
                    sbt([128, S], BF16) as pi_sb,
                    sbt([128, 16 * S], BF16) as pT,
                    sbt([128, 128], F32) as anat,
                    sbt([128, 128], F32) as anat2,
                ):
                    t_attn_done = 0
                    for h in range(NHH):
                        qs = q0 if h % 2 == 0 else q1
                        ks = k0 if h % 2 == 0 else k1
                        vs = v0 if h % 2 == 0 else v1
                        w("tt", t_attn_done, "sync")
                        vq = ld(qs[:, :].rearrange("p (t d) -> p t d", t=NST), q_h[:, h, :].rearrange("(t p) d -> p t d", p=128))
                        vk = ld(ks[:, :].rearrange("p (t d) -> p t d", t=NST), k_h[:, h, :].rearrange("(t p) d -> p t d", p=128))
                        vv_ = ld(vs[:, :].rearrange("p (t d) -> p t d", t=NST), v_h[:, h, :].rearrange("(t p) d -> p t d", p=128))
                        w("din", vv_, "tensor")
                        w("ss", ctr["ss"], "tensor")
                        for which, src_sb, dst_sb in (("q", qs, qT), ("k", ks, kT)):
                            for g in range(4):
                                for t in range(4):
                                    if T:
                                        e.transpose(ps_t[g % 2][:, t * 128:(t + 1) * 128],
                                                    src_sb[:, (4 * g + t) * 128:(4 * g + t + 1) * 128],
                                                    sb["eye_b"][:, :])
                                tv = bump("tt", "tensor")
                                w("tt", tv, "scalar")
                                if SC:
                                    e.copy(dst_sb[:, 4 * g * 128:(4 * g + 4) * 128],
                                           ps_t[g % 2][:, :])
                                ev = bump("ss", "scalar")
                                if g >= 1:
                                    pass
                                w("ss", ev, "tensor")
                        w("tt", t_attn_done, "vector")
                        if V:
                            for c in range(4):
                                for t in range(4 * c + 1, 4 * c + 4):
                                    e.memset(pT[:, t * S + c * 512: t * S + t * 128], 0.0)
                        bump("vv", "vector")
                        for j in range(NST):
                            nval = (j + 1) * 128
                            w("vv", ctr["vv"], "tensor")
                            w("ss", ctr["ss"], "tensor")
                            for c in range((nval + 511) // 512):
                                wid = min(512, nval - c * 512)
                                if T:
                                    e.matmul(ps_a[:, c * 512:c * 512 + wid],
                                             qT[:, j * 128:(j + 1) * 128],
                                             kT[:, c * 512:c * 512 + wid],
                                             start=True, stop=True)
                            smm = bump("tt", "tensor")
                            w("tt", smm, "vector")
                            if V:
                                e.tensor_tensor(ps_a[:, j * 128:j * 128 + 128],
                                                ps_a[:, j * 128:j * 128 + 128],
                                                sb["dmask"][:, :], OP.add)
                                e.reduce_max(tny[:, 14:15], ps_a[:, 0:nval], AX,
                                             negate=True)
                                e.tensor_scalar(tny[:, 15:16], tny[:, 14:15],
                                                sb["cq"][:, j * NHH + h:j * NHH + h + 1],
                                                sb["Lc"][:, 0:1], OP.mult, OP.add)
                            mv = bump("vv", "vector")
                            w("vv", mv, "scalar")
                            if SC:
                                e.activation(e_sb[:, 0:nval], ps_a[:, 0:nval], AF.Exp,
                                             bias=tny[:, 15:16],
                                             scale=sb["cq"][:, j * NHH + h:j * NHH + h + 1],
                                             accum_out=tny[:, 16:17])
                            ev = bump("ss", "scalar")
                            w("ss", ev, "vector")
                            if V:
                                e.reciprocal(sb["invz"][:, j:j + 1], tny[:, 16:17])
                                e.tensor_copy(md_sb[:, 0:nval].bitcast(I32), e_sb[:, 0:nval])
                                e.tensor_copy(pi_sb[:, 0:nval], md_sb[:, 0:nval].bitcast(I32))
                            pv = bump("vv", "vector")
                            w("vv", pv, "tensor")
                            for g in range((j + 1 + 3) // 4):
                                tcnt = min(4, j + 1 - 4 * g)
                                for t in range(tcnt):
                                    if T:
                                        e.transpose(ps_t[g % 2][:, t * 128:(t + 1) * 128],
                                                    pi_sb[:, (4 * g + t) * 128:(4 * g + t + 1) * 128],
                                                    sb["eye_b"][:, :])
                                tv = bump("tt", "tensor")
                                w("tt", tv, "scalar")
                                if SC:
                                    e.copy(pT[:, :].rearrange("p (t s) -> p t s", t=16)
                                           [:, 4 * g:4 * g + tcnt, j * 128:(j + 1) * 128],
                                           ps_t[g % 2][:, 0:tcnt * 128]
                                           .rearrange("p (t s) -> p t s", t=tcnt))
                                ev = bump("ss", "scalar")
                                w("ss", ev, "tensor")
                        w("ss", ctr["ss"], "tensor")
                        for sk in range(16):
                            for c in range(sk // 4, 4):
                                if T:
                                    e.matmul(ps_a[:, c * 512:(c + 1) * 512],
                                             vs[:, sk * 128:(sk + 1) * 128],
                                             pT[:, sk * S + c * 512:sk * S + (c + 1) * 512],
                                             start=(sk == 0), stop=(sk == 4 * c + 3))
                            bump("tt", "tensor")
                        av_done = ctr["tt"]
                        for j in range(NST):
                            w("tt", av_done, "scalar")
                            w("dst", ctr["dst"], "scalar")
                            if SC:
                                e.copy((anat if j % 2 == 0 else anat2)[:, :],
                                       ps_a[:, j * 128:(j + 1) * 128])
                            cv = bump("ss", "scalar")
                            w("ss", cv, "tensor")
                            if T:
                                e.transpose(ps_b[:, (j % 2) * 128:(j % 2) * 128 + 128],
                                            (anat if j % 2 == 0 else anat2)[:, :],
                                            sb["eye_f"][:, :])
                            tv = bump("tt", "tensor")
                            w("tt", tv, "scalar")
                            if SC:
                                e.mul((anat if j % 2 == 0 else anat2)[:, :],
                                      ps_b[:, (j % 2) * 128:(j % 2) * 128 + 128],
                                      sb["invz"][:, j:j + 1])
                            mv = bump("ss", "scalar")
                            w("ss", mv, "gpsimd")
                            st(cc_i[j * 128:(j + 1) * 128, h * 128:(h + 1) * 128],
                               (anat if j % 2 == 0 else anat2)[:, :])
                        t_attn_done = ctr["tt"]
            if part == 1 and 5 in PHASES:
                _p5()
            if part == 1:
                w("dst", ctr["dst"], "gpsimd")
            barrier()

            def _p6():
                with (
                    sbt([128, 32 * 1024], BF16) as xbT,
                    sbt([128, DM], F32) as scBo,
                ):
                  with (
                    sbt([128, DM], F32) as f0,
                    sbt([128, DM], F32) as f1,
                    sbt([128, DM], F32) as fc,
                    sbt([128, DM], F32) as fw,
                    sbt([128, DM], BF16) as fb,
                  ):
                      for ct in range(32):
                          ws = f0 if ct % 2 == 0 else f1
                          if ct >= 2:
                              w("vv", ctr["vv"], "sync")
                          vl = ld(ws[:, :], wo_n[ct * 128:(ct + 1) * 128, :])
                          w("din", vl, "vector")
                          if V:
                              e.reduce_max(tny[:, 9:10], ws[:, :], AX,
                                           apply_absolute_value=True)
                              e.tensor_scalar(sb["wsc"][:, ct:ct + 1], tny[:, 9:10],
                                              sb["Lc"][:, 3:4], 1e-8, OP.mult, OP.max)
                              e.reciprocal(sb["wsc"][:, 32 + ct:32 + ct + 1],
                                           sb["wsc"][:, ct:ct + 1])
                          fv = bump("vv", "vector")
                          w("vv", fv, "gpsimd")
                          st(sc_h[ct * 128:(ct + 1) * 128], sb["wsc"][:, ct:ct + 1])
                          st(sc_h[DM + ct * 128:DM + (ct + 1) * 128],
                             sb["wsc"][:, 32 + ct:32 + ct + 1])
                      w("dst", ctr["dst"], "sync")
                      vl = ld(sb["srow"][0:1, :], sc_h[0:DM])
                      w("din", vl, "tensor")
                      for cblk in range(8):
                          w("ss", ctr["ss"], "tensor")
                          if T:
                              e.matmul(ps_b[:, 0:512], sb["ones"][0:1, :],
                                       sb["srow"][0:1, cblk * 512:(cblk + 1) * 512],
                                       start=True, stop=True)
                          bump("tt", "tensor")
                          w("tt", ctr["tt"], "scalar")
                          if SC:
                              e.copy(scBo[:, cblk * 512:(cblk + 1) * 512], ps_b[:, 0:512])
                          bump("ss", "scalar")
                      invBo_row_ready = ctr["ss"]
                      for i in range(NSTH):
                          xt = f0 if i % 2 == 0 else f1
                          w("vv", ctr["vv"], "sync")
                          vl = ld(xt[:, :], xh_in[i * 128:(i + 1) * 128, :])
                          w("din", vl, "vector")
                          if V:
                              e.reduce_sum(tny[:, 1:2], xt[:, :], AX)
                              e.tensor_single_scalar(tny[:, 2:3], tny[:, 1:2],
                                                     -1.0 / float(DM), OP.mult)
                          bump("vv", "vector")
                          w("vv", ctr["vv"], "scalar")
                          if SC:
                              e.activation(fc[:, :], xt[:, :], AF.Identity, bias=tny[:, 2:3])
                              e.activation(fw[:, :], fc[:, :], AF.Square,
                                           accum_out=tny[:, 3:4])
                          bump("ss", "scalar")
                          w("ss", ctr["ss"], "vector")
                          if V:
                              e.tensor_scalar(tny[:, 4:5], tny[:, 3:4], 1.0 / float(DM),
                                              1e-5, OP.mult, OP.add)
                          bump("vv", "vector")
                          w("vv", ctr["vv"], "scalar")
                          if SC:
                              e.activation(tny[:, 5:6], tny[:, 4:5], AF.Sqrt)
                          bump("ss", "scalar")
                          w("ss", ctr["ss"], "vector")
                          if V:
                              e.reciprocal(tny[:, 6:7], tny[:, 5:6])
                          bump("vv", "vector")
                          w("vv", ctr["vv"], "scalar")
                          if SC:
                              e.activation(fb[:, :], fc[:, :], AF.Identity,
                                           scale=tny[:, 6:7])
                          fbv = bump("ss", "scalar")
                          w("ss", fbv, "tensor")
                          evac = {}
                          for d in range(32):
                              if d >= 2:
                                  w("ss", evac[d - 2], "tensor")
                              if T:
                                  e.transpose(ps_t[d % 2][:, 0:128],
                                              fb[:, d * 128:(d + 1) * 128], sb["eye_b"][:, :])
                              tv = bump("tt", "tensor")
                              w("tt", tv, "scalar")
                              if SC:
                                  e.copy(xbT[:, d * 1024 + i * 128:d * 1024 + (i + 1) * 128],
                                         ps_t[d % 2][:, 0:128])
                              evac[d] = bump("ss", "scalar")
                  with sbt([128, 32 * 1024], BF16) as idq, sbt([128, 1024], BF16) as gq0, sbt([128, 1024], BF16) as gq1, sbt([128, 1024], F32) as id0, sbt([128, 1024], F32) as id1:
                        for Q in range(4):
                            for dtl in range(32):
                                ws = id0 if dtl % 2 == 0 else id1
                                w("ss", ctr["ss"], "sync")
                                vl = ld(ws[:, :], id_t[dtl * 128:(dtl + 1) * 128,
                                                            Q * 1024:(Q + 1) * 1024])
                                w("din", vl, "scalar")
                                if SC:
                                    e.copy(idq[:, dtl * 1024:(dtl + 1) * 1024], ws[:, :])
                                bump("ss", "scalar")
                            idq_done = ctr["ss"]
                            for i in range(NSTH):
                                gq = gq0 if i % 2 == 0 else gq1
                                w("ss", idq_done, "tensor")
                                w("ss", ctr["ss"], "tensor")
                                for d in range(32):
                                    for half in range(2):
                                        if T:
                                            e.matmul(ps_b[:, half * 512:(half + 1) * 512],
                                                     xbT[:, d * 1024 + i * 128:d * 1024 + (i + 1) * 128],
                                                     idq[:, d * 1024 + half * 512:d * 1024 + (half + 1) * 512],
                                                     start=(d == 0), stop=(d == 31))
                                mmv = bump("tt", "tensor")
                                w("tt", mmv, "scalar")
                                if (Q * NSTH + i) >= 2:
                                    w("dst", ctr["dst"], "scalar")
                                if SC:
                                    e.activation(gq[:, :], ps_b[:, 0:1024], AF.Silu)
                                gv = bump("ss", "scalar")
                                st(g_hbm[i * 128:(i + 1) * 128, Q * 1024:(Q + 1) * 1024],
                                   gq[:, :], q="scalar")
                w("dsc", ctr["dsc"], "scalar")
                barrier()
            if part == 2 and 6 in PHASES:
                _p6()
            def _p7():
                with (
                    sbt([128, 32 * 1024], BF16) as aqT,
                    sbt([128, 32 * 1024], BF16) as wTo,
                ):

                    with (
                        sbt([128, DM], F32) as aq_f,
                        sbt([128, DM], F32) as aw,
                        sbt([128, DM], BF16) as aqi,
                    ):
                        for i in range(NSTH):
                            if i >= 1:
                                w("vv", ctr["vv"], "sync")
                            v2_ = ld(aq_f[:, :], attn_in[i * 128:(i + 1) * 128, :])
                            w("din", v2_, "vector")
                            w("tt", ctr["tt"], "vector")
                            if V:
                                e.reduce_max(tny[:, 7:8], aq_f[:, :], AX,
                                             apply_absolute_value=True)
                                e.tensor_scalar(sb["sca"][:, i:i + 1], tny[:, 7:8],
                                                sb["Lc"][:, 1:2], 1e-8, OP.mult, OP.max)
                                e.reciprocal(tny[:, 9:10], tny[:, 7:8])
                                e.tensor_scalar(aw[:, :], aq_f[:, :], 127.0, tny[:, 9:10],
                                                OP.mult, OP.mult)
                                e.tensor_copy(aq_f[:, :].bitcast(I32), aw[:, :])
                                e.tensor_copy(aqi[:, :], aq_f[:, :].bitcast(I32))
                            qv = bump("vv", "vector")
                            w("vv", qv, "tensor")
                            w("ss", ctr["ss"], "tensor")
                            evac = {}
                            for d in range(32):
                                if d >= 2:
                                    w("ss", evac[d - 2], "tensor")
                                if T:
                                    e.transpose(ps_t[d % 2][:, 0:128],
                                                aqi[:, d * 128:(d + 1) * 128], sb["eye_b"][:, :])
                                tv = bump("tt", "tensor")
                                w("tt", tv, "scalar")
                                if SC:
                                    e.copy(aqT[:, d * 1024 + i * 128:d * 1024 + (i + 1) * 128],
                                           ps_t[d % 2][:, 0:128])
                                evac[d] = bump("ss", "scalar")
                        aqT_done = ctr["ss"]
    
                    with (
                        sbt([128, 1024], F32) as wq0,
                        sbt([128, 1024], F32) as wq1,
                        sbt([128, 1024], F32) as invBq,
                        sbt([128, 1024], F32) as scQ,
                        sbt([128, 1024], F32) as s_sb,
                        sbt([128, 1024], BF16) as g_sb,
                        sbt([128, 1024], F32) as xh_sb,
                        sbt([128, 1024], F32) as o_sb,
                    ):
                        for Q in range(4):
                            w("tt", ctr["tt"], "vector")
                            w("dst", ctr["dst"], "vector")
                            w("ss", ctr["ss"], "sync")
                            vinv = ld(sb["srow"][0:1, 0:1024],
                                      sc_h[DM + Q * 1024:DM + (Q + 1) * 1024])
                            vinv = ld(sb["srow"][0:1, 1024:2048],
                                      sc_h[Q * 1024:(Q + 1) * 1024])
                            w("din", vinv, "tensor")
                            for hblk in range(4):
                                dst_t = invBq if hblk < 2 else scQ
                                w("ss", ctr["ss"], "tensor")
                                w("vv", ctr["vv"], "tensor")
                                if T:
                                    e.matmul(ps_b[:, 0:512], sb["ones"][0:1, :],
                                             sb["srow"][0:1, hblk * 512:(hblk + 1) * 512],
                                             start=True, stop=True)
                                bump("tt", "tensor")
                                w("tt", ctr["tt"], "scalar")
                                if SC:
                                    e.copy(dst_t[:, (hblk % 2) * 512:(hblk % 2 + 1) * 512],
                                           ps_b[:, 0:512])
                                bump("ss", "scalar")
                            invq_done = ctr["ss"]
                            scq_done = ctr["ss"]
                            for rt in range(32):
                                ws = wq0 if rt % 2 == 0 else wq1
                                w("vv", ctr["vv"], "sync")
                                vl = ld(ws[:, :], wo_t[rt * 128:(rt + 1) * 128,
                                                       Q * 1024:(Q + 1) * 1024])
                                w("din", vl, "vector")
                                w("ss", invq_done, "vector")
                                if V:
                                    e.tensor_tensor(s_sb[:, :], ws[:, :], invBq[:, :], OP.mult)
                                    e.tensor_copy(o_sb[:, :].bitcast(I32), s_sb[:, :])
                                    e.tensor_copy(wTo[:, rt * 1024:(rt + 1) * 1024],
                                                  o_sb[:, :].bitcast(I32))
                                bump("vv", "vector")
                            wo_ready = ctr["vv"]
                            for i in range(NSTH):
                                w("ss", aqT_done, "tensor")
                                w("vv", wo_ready, "tensor")
                                if True:
                                    w("vv", ctr["vv"], "tensor")
                                for d in range(32):
                                    for half in range(2):
                                        if T:
                                            e.matmul(ps_b[:, half * 512:(half + 1) * 512],
                                                     aqT[:, d * 1024 + i * 128:d * 1024 + (i + 1) * 128],
                                                     wTo[:, d * 1024 + half * 512:d * 1024 + (half + 1) * 512],
                                                     start=(d == 0), stop=(d == 31))
                                mmv = bump("tt", "tensor")
                                w("vv", ctr["vv"], "sync")
                                vg = ld(g_sb[:, :], g_hbm[i * 128:(i + 1) * 128,
                                                          Q * 1024:(Q + 1) * 1024])
                                vx = ld(xh_sb[:, :], xh_in[i * 128:(i + 1) * 128,
                                                           Q * 1024:(Q + 1) * 1024])
                                w("tt", mmv, "vector")
                                w("din", vx, "vector")
                                w("ss", scq_done, "vector")
                                w("dst", ctr["dst"], "vector")
                                if V:
                                    for half in range(2):
                                        e.scalar_tensor_tensor(
                                            s_sb[:, half * 512:(half + 1) * 512],
                                            ps_b[:, half * 512:(half + 1) * 512],
                                            sb["sca"][:, i:i + 1],
                                            scQ[:, half * 512:(half + 1) * 512],
                                            OP.mult, OP.mult)
                                    e.tensor_tensor(o_sb[:, :], s_sb[:, :], g_sb[:, :], OP.mult)
                                    e.tensor_tensor(o_sb[:, :], o_sb[:, :], xh_sb[:, :], OP.add)
                                ov = bump("vv", "vector")
                                w("vv", ov, "gpsimd")
                                st(y_out[i * 128:(i + 1) * 128, Q * 1024:(Q + 1) * 1024],
                                   o_sb[:, :])
            if part == 2 and 7 in PHASES:
                _p7()
            w("dst", ctr["dst"], "gpsimd")
            barrier()

    with nc.Block() as blk:
        blk.sync(lambda e: emit_all("sync", e))
        blk.vector(lambda e: emit_all("vector", e))
        blk.scalar(lambda e: emit_all("scalar", e))
        blk.tensor(lambda e: emit_all("tensor", e))
        blk.gpsimd(lambda e: emit_all("gpsimd", e))
    stack.close()
    return nc


def _rope_tables():
    import jax
    import jax.numpy as jnp
    cpu = jax.local_devices(backend="cpu")[0]
    with jax.default_device(cpu):
        inv_freq = 1.0 / 10000.0 ** (jnp.arange(0, DH, 2, dtype=jnp.float32) / DH)
        t = jnp.arange(S, dtype=jnp.float32)
        freqs = jnp.einsum("i,j->ij", t, inv_freq)
        emb = jnp.concatenate([freqs, freqs], axis=-1)
        cos = np.asarray(jnp.cos(emb), dtype=np.float32)
        sin = np.asarray(jnp.sin(emb), dtype=np.float32)
    ss = np.concatenate([-sin[:, :64], sin[:, 64:]], axis=-1).astype(np.float32)
    return np.ascontiguousarray(cos), np.ascontiguousarray(ss)


def kernel(L0_attn_hidden_states, L0_attn_wq, L0_attn_wk, L0_attn_wv, L0_attn_wo,
           L0_ffn_identity):
    x = np.ascontiguousarray(np.asarray(L0_attn_hidden_states, dtype=np.float32))
    wq = np.asarray(L0_attn_wq, dtype=np.float32)
    wk = np.asarray(L0_attn_wk, dtype=np.float32)
    wv = np.asarray(L0_attn_wv, dtype=np.float32)
    wo = np.asarray(L0_attn_wo, dtype=np.float32)
    ident = np.asarray(L0_ffn_identity, dtype=np.float32)

    cos, ss = _rope_tables()
    dmask = np.where(np.arange(128)[:, None] >= np.arange(128)[None, :],
                     np.float32(0.0), np.float32(-1.0e30)).astype(np.float32)
    eye_b = np.eye(128, dtype=bf16)
    eye_f = np.eye(128, dtype=np.float32)
    ones_f = np.ones((1, 128), dtype=np.float32)
    wqT = np.ascontiguousarray(wq.T)
    wkT = np.ascontiguousarray(wk.T)
    wvT = np.ascontiguousarray(wv.T)
    woT = np.ascontiguousarray(wo.T)
    idT = np.ascontiguousarray(ident.T)

    nc1 = build_nc(1)
    nc2 = build_nc(2)
    in_maps = []
    for c in range(8):
        b, H = c // 2, c % 2
        sl = slice(H * CHH, (H + 1) * CHH)
        in_maps.append({
            "x": np.ascontiguousarray(x[b]),
            "x_half": np.ascontiguousarray(x[b][H * 1024:(H + 1) * 1024]),
            "wq_n": np.ascontiguousarray(wq[sl]),
            "wk_n": np.ascontiguousarray(wk[sl]),
            "wv_n": np.ascontiguousarray(wv[sl]),
            "wq_t": np.ascontiguousarray(wqT[:, sl]),
            "wk_t": np.ascontiguousarray(wkT[:, sl]),
            "wv_t": np.ascontiguousarray(wvT[:, sl]),
            "wo_n": wo,
            "wo_t": woT,
            "id_t": idT,
            "cos_t": cos,
            "ssin_t": ss,
            "dmask": dmask,
            "eye_b": eye_b,
            "eye_f": eye_f,
            "ones_f": ones_f,
        })
    res1 = run_bass_kernel_spmd(nc1, in_maps, core_ids=list(range(8)))
    for c in range(8):
        b, H = c // 2, c % 2
        full = np.concatenate([res1.results[2 * b]["attn"],
                               res1.results[2 * b + 1]["attn"]], axis=1)
        in_maps[c]["attn_full"] = np.ascontiguousarray(
            full[H * 1024:(H + 1) * 1024, :])
    res2 = run_bass_kernel_spmd(nc2, in_maps, core_ids=list(range(8)))
    out = np.empty((4, S, DM), dtype=np.float32)
    for c in range(8):
        b, H = c // 2, c % 2
        out[b, H * 1024:(H + 1) * 1024, :] = res2.results[c]["y"]
    return out
